# version 02
# baseline (speedup 1.0000x reference)
import numpy as np
from contextlib import ExitStack

import concourse.bass as bass
import concourse.mybir as mybir
from concourse.bass_utils import run_bass_kernel_spmd

F32 = mybir.dt.float32
BF16 = mybir.dt.bfloat16
AF = mybir.ActivationFunctionType
ALU = mybir.AluOpType
AX = mybir.AxisListType.X

B, S, H = 4, 4096, 2048
DFF = 8192
EPS_LN = 1e-5
EPS_Q = 1e-5
T = 2048
TQ = 512
NTILE = T // TQ
NC = TQ // 128
J = 4096
HC = H // 128
JB = 256
NJB = J // JB
JCH = J // 128
RSQ_H = float(1.0 / np.sqrt(np.float32(H), dtype=np.float32))
N_CORES = 8
FSL = 1024
NSL = DFF // FSL
FCS = FSL // 128


class Sem:
    def __init__(self, h):
        self.h, self.n = h, 0


class _StopEmission(Exception):
    pass


def build(tap=None, keep=False):
    nc = bass.Bass("TRN2", target_bir_lowering=False)

    xo = nc.dram_tensor("xo", [T, H], F32, kind="ExternalInput").ap()
    xf = nc.dram_tensor("xf", [J, H], F32, kind="ExternalInput").ap()
    wqkv = nc.dram_tensor("wqkv", [3 * H, H], F32, kind="ExternalInput").ap()
    bqkv = nc.dram_tensor("bqkv", [3 * H], F32, kind="ExternalInput").ap()
    wout = nc.dram_tensor("wout", [H, H], F32, kind="ExternalInput").ap()
    bout = nc.dram_tensor("bout", [H], F32, kind="ExternalInput").ap()
    w1 = nc.dram_tensor("w1", [DFF, H], F32, kind="ExternalInput").ap()
    w2 = nc.dram_tensor("w2", [DFF, H], F32, kind="ExternalInput").ap()
    w3 = nc.dram_tensor("w3", [H, DFF], F32, kind="ExternalInput").ap()
    out = nc.dram_tensor("out", [T, H], F32, kind="ExternalOutput").ap()

    xf_hi = nc.dram_tensor("xf_hi", [J, H], BF16, kind="Internal").ap()
    xf_lo = nc.dram_tensor("xf_lo", [J, H], BF16, kind="Internal").ap()
    xo_hi = nc.dram_tensor("xo_hi", [T, H], BF16, kind="Internal").ap()
    xo_lo = nc.dram_tensor("xo_lo", [T, H], BF16, kind="Internal").ap()
    wq_fm = [nc.dram_tensor(f"wq_fm_{p}", [H, H], BF16, kind="Internal").ap() for p in "hl"]
    wk_fm = [nc.dram_tensor(f"wk_fm_{p}", [H, H], BF16, kind="Internal").ap() for p in "hl"]
    wv_fm = [nc.dram_tensor(f"wv_fm_{p}", [H, H], BF16, kind="Internal").ap() for p in "hl"]
    wo_fm = [nc.dram_tensor(f"wo_fm_{p}", [H, H], BF16, kind="Internal").ap() for p in "hl"]
    wqkv_tm = [nc.dram_tensor(f"wqkv_tm_{p}", [3 * H, H], BF16, kind="Internal").ap() for p in "hl"]
    wo_tm = [nc.dram_tensor(f"wo_tm_{p}", [H, H], BF16, kind="Internal").ap() for p in "hl"]
    sg1_tm = nc.dram_tensor("sg1_tm", [DFF, H], BF16, kind="Internal").ap()
    sg2_tm = nc.dram_tensor("sg2_tm", [DFF, H], BF16, kind="Internal").ap()
    sg3_tm = nc.dram_tensor("sg3_tm", [H, DFF], BF16, kind="Internal").ap()
    sg1_fm = nc.dram_tensor("sg1_fm", [H, DFF], BF16, kind="Internal").ap()
    sg2_fm = nc.dram_tensor("sg2_fm", [H, DFF], BF16, kind="Internal").ap()
    sg3_fm = nc.dram_tensor("sg3_fm", [DFF, H], BF16, kind="Internal").ap()
    pfm_sc = [nc.dram_tensor(f"pfm_{p}", [J, TQ], BF16, kind="Internal").ap() for p in "hl"]
    bias_sc = nc.dram_tensor("bias_sc", [6, H], BF16, kind="Internal").ap()

    dbg = None
    if tap is not None:
        dbg = nc.dram_tensor("dbg", list(tap[2]), F32, kind="ExternalOutput").ap()

    es = ExitStack()
    sbt = lambda nm, shp, dt=F32: es.enter_context(nc.sbuf_tensor(nm, shp, dt))

    F1 = sbt("F1", [128, 16384])
    F2 = sbt("F2", [128, 8192])
    B1 = sbt("B1", [128, 16384], BF16)
    B2 = sbt("B2", [128, 16384], BF16)
    B3 = sbt("B3", [128, 20480], BF16)
    S8 = sbt("S8", [128, 1536])
    ONESB = sbt("ONESB", [1, 128], BF16)
    ONESF = sbt("ONESF", [128, 128])
    STAT = sbt("STAT", [128, 40])
    NMAX = sbt("NMAX", [128, 4])
    RSUM = sbt("RSUM", [128, 4])
    MU = sbt("MU", [128, 4])
    RSTD = sbt("RSTD", [128, 4])
    GMX = sbt("GMX", [128, 64])
    USC = sbt("USC", [128, 64])
    VSC = sbt("VSC", [128, 64])
    SW = sbt("SW", [128, 4])

    PS = [es.enter_context(nc.psum_tensor(f"ps{i}", [128, 512], mybir.dt.float32))
          for i in range(4)]
    PBIG = es.enter_context(nc.psum_tensor("pbig", [128, 2048], mybir.dt.float32))

    with nc.semaphore("ld") as _ld, nc.semaphore("gp") as _gp, \
         nc.semaphore("sc") as _sc, nc.semaphore("mm") as _mm, \
         nc.semaphore("ev") as _ev, nc.semaphore("vx") as _vx, \
         nc.semaphore("st") as _st, nc.Block() as block:

        ld, gp, sc = Sem(_ld), Sem(_gp), Sem(_sc)
        mm, ev, vx, st = Sem(_mm), Sem(_ev), Sem(_vx), Sem(_st)

        steps = {k: [] for k in ("sync", "gpsimd", "vector", "scalar", "tensor")}

        def on(eng, fn):
            steps[eng].append(fn)

        ps_free = [(ev.h, 0)] * 4
        pbig_free = [(vx.h, 0)]

        def dap(apobj, offset, dims):
            return bass.AP(apobj.tensor, offset, dims)

        def p_init(vector):
            vector.memset(ONESB[:], 1.0)
            vector.memset(ONESF[:], 1.0).then_inc(vx.h, 1)
        vx.n += 1
        on("vector", p_init)

        for bi, (bsrc, lo0) in enumerate([(bqkv, 0), (bqkv, 2 * H), (bout, 0)]):
            w_gp = gp.n

            def pb_ld(sync, bsrc=bsrc, lo0=lo0, w_gp=w_gp):
                sync.wait_ge(gp.h, w_gp)
                sync.dma_start(out=F2[0:1, 0:H], in_=bsrc[lo0:lo0 + H].unsqueeze(0)).then_inc(ld.h, 16)
            ld.n += 16
            on("sync", pb_ld)
            w_ld = ld.n

            def pb_sp(vector, w_ld=w_ld, w_self=vx.n):
                vector.wait_ge(ld.h, w_ld)
                vector.tensor_copy(B3[0:1, 16384:16384 + H], F2[0:1, 0:H]).then_inc(vx.h, 1)
                vector.wait_ge(vx.h, w_self + 1)
                vector.tensor_tensor(B3[0:1, 18432:18432 + H], F2[0:1, 0:H],
                                     B3[0:1, 16384:16384 + H],
                                     ALU.subtract).then_inc(vx.h, 1)
            vx.n += 2
            on("vector", pb_sp)
            w_vx = vx.n

            def pb_st(gpsimd, bi=bi, w_vx=w_vx):
                gpsimd.wait_ge(vx.h, w_vx)
                gpsimd.dma_start(out=bias_sc[2 * bi:2 * bi + 1, :],
                                 in_=B3[0:1, 16384:16384 + H]).then_inc(gp.h, 16)
                gpsimd.dma_start(out=bias_sc[2 * bi + 1:2 * bi + 2, :],
                                 in_=B3[0:1, 18432:18432 + H]).then_inc(gp.h, 16)
            gp.n += 32
            on("gpsimd", pb_st)

        def emit_split(src, n_rows, row_w, dst_hi, dst_lo, sign_dst=None, wabs_col0=None):
            rpb = (1024 * 1024) // row_w
            for b in range(n_rows // rpb):
                r0 = b * rpb
                w = (vx.n, ev.n, gp.n)

                def s_ld(sync, r0=r0, w=w, src=src, rpb=rpb):
                    sync.wait_ge(vx.h, w[0])
                    sync.wait_ge(ev.h, w[1])
                    sync.wait_ge(gp.h, w[2])
                    sync.dma_start(
                        out=F1[:, 0:8192],
                        in_=src[r0:r0 + rpb, :].rearrange("(p a) b -> p (a b)", p=128),
                    ).then_inc(ld.h, 16)
                ld.n += 16
                on("sync", s_ld)
                w_ld = ld.n

                if sign_dst is None:
                    def s_sp(vector, w_ld=w_ld):
                        vector.wait_ge(ld.h, w_ld)
                        vector.tensor_copy(B2[:, 0:8192], F1[:, 0:8192])
                        vector.tensor_tensor(B2[:, 8192:16384], F1[:, 0:8192],
                                             B2[:, 0:8192], ALU.subtract).then_inc(vx.h, 1)
                    vx.n += 1
                    on("vector", s_sp)
                    w_vx = vx.n

                    def s_st(gpsimd, r0=r0, w_vx=w_vx, dst_hi=dst_hi, dst_lo=dst_lo, rpb=rpb):
                        gpsimd.wait_ge(vx.h, w_vx)
                        gpsimd.dma_start(
                            out=dst_hi[r0:r0 + rpb, :].rearrange("(p a) b -> p (a b)", p=128),
                            in_=B2[:, 0:8192]).then_inc(gp.h, 16)
                        gpsimd.dma_start(
                            out=dst_lo[r0:r0 + rpb, :].rearrange("(p a) b -> p (a b)", p=128),
                            in_=B2[:, 8192:16384]).then_inc(gp.h, 16)
                    gp.n += 32
                    on("gpsimd", s_st)
                else:
                    wc = wabs_col0 + b

                    def s_sg(scalar, w_ld=w_ld, wc=wc):
                        scalar.wait_ge(ld.h, w_ld)
                        scalar.activation(B2[:, 0:8192], F1[:, 0:8192], AF.Sign)
                        scalar.activation(F2[:, 0:8192], F1[:, 0:8192], AF.Abs,
                                          accum_out=GMX[:, wc:wc + 1]).then_inc(ev.h, 1)
                    ev.n += 1
                    on("scalar", s_sg)
                    w_ev = ev.n

                    def s_st(gpsimd, r0=r0, w_ev=w_ev, sign_dst=sign_dst, rpb=rpb):
                        gpsimd.wait_ge(ev.h, w_ev)
                        gpsimd.dma_start(
                            out=sign_dst[r0:r0 + rpb, :].rearrange("(p a) b -> p (a b)", p=128),
                            in_=B2[:, 0:8192]).then_inc(gp.h, 16)
                    gp.n += 16
                    on("gpsimd", s_st)

        emit_split(xo, T, H, xo_hi, xo_lo)
        emit_split(xf, J, H, xf_hi, xf_lo)
        emit_split(wqkv, 3 * H, H, wqkv_tm[0], wqkv_tm[1])
        emit_split(wout, H, H, wo_tm[0], wo_tm[1])

        def emit_bounce(jobs):
            for (src, row0, dst, n_o, n_h) in jobs:
                for hcb in range(n_h // 128):
                    for c0 in range(0, n_o, 8192):
                        cw = min(8192, n_o - c0)
                        w_gp = gp.n

                        def b_tp(scalar, src=src, row0=row0, hcb=hcb, c0=c0, cw=cw, w_gp=w_gp):
                            scalar.wait_ge(gp.h, w_gp)
                            scalar.dma_start_transpose(
                                B3[:, 0:cw],
                                src[row0 + c0:row0 + c0 + cw, hcb * 128:(hcb + 1) * 128],
                            ).then_inc(sc.h, 16)
                        sc.n += 16
                        on("scalar", b_tp)
                        w_sc = sc.n

                        def b_st(gpsimd, dst=dst, hcb=hcb, c0=c0, cw=cw, w_sc=w_sc):
                            gpsimd.wait_ge(sc.h, w_sc)
                            gpsimd.dma_start(out=dst[hcb * 128:(hcb + 1) * 128, c0:c0 + cw],
                                             in_=B3[:, 0:cw]).then_inc(gp.h, 16)
                        gp.n += 16
                        on("gpsimd", b_st)

        emit_bounce([(wqkv_tm[0], 0, wq_fm[0], H, H), (wqkv_tm[1], 0, wq_fm[1], H, H),
                     (wqkv_tm[0], 2 * H, wv_fm[0], H, H), (wqkv_tm[1], 2 * H, wv_fm[1], H, H),
                     (wo_tm[0], 0, wo_fm[0], H, H), (wo_tm[1], 0, wo_fm[1], H, H)])

        emit_split(w1, DFF, H, None, None, sign_dst=sg1_tm, wabs_col0=0)
        emit_split(w2, DFF, H, None, None, sign_dst=sg2_tm, wabs_col0=16)
        emit_split(w3, H, DFF, None, None, sign_dst=sg3_tm, wabs_col0=32)

        emit_bounce([(sg1_tm, 0, sg1_fm, DFF, H), (sg2_tm, 0, sg2_fm, DFF, H),
                     (sg3_tm, 0, sg3_fm, H, DFF)])

        for wi in range(3):
            w_ev = ev.n

            def w_red(vector, wi=wi, w_ev=w_ev):
                vector.wait_ge(ev.h, w_ev)
                vector.reduce_sum(STAT[:, wi:wi + 1], GMX[:, wi * 16:wi * 16 + 16],
                                  AX).then_inc(vx.h, 1)
            vx.n += 1
            on("vector", w_red)
            w_vx = vx.n
            w_fr = ps_free[3]

            def w_mm(tensor, wi=wi, w_vx=w_vx, w_fr=w_fr):
                tensor.wait_ge(vx.h, w_vx)
                tensor.wait_ge(w_fr[0], w_fr[1])
                tensor.matmul(PS[3][:, 0:1], ONESF[:], STAT[:, wi:wi + 1],
                              start=True, stop=True).then_inc(mm.h, 1)
            mm.n += 1
            on("tensor", w_mm)
            w_mmn = mm.n

            def w_ev2(scalar, wi=wi, w_mmn=w_mmn):
                scalar.wait_ge(mm.h, w_mmn)
                scalar.activation(SW[:, wi:wi + 1], PS[3][:, 0:1], AF.Copy,
                                  scale=1.0 / (DFF * H)).then_inc(ev.h, 1)
            ev.n += 1
            on("scalar", w_ev2)
            ps_free[3] = (ev.h, ev.n)

        TAPD = {"on": tap is not None, "name": tap[0] if tap else None,
                "tile": tap[1] if tap else -1, "done": False}

        def tap_hit(name, ti):
            return TAPD["on"] and not TAPD["done"] and TAPD["name"] == name and TAPD["tile"] == ti

        def emit_tapbuf(buf, c0, cw):
            TAPD["done"] = True
            w = (vx.n, ev.n, mm.n)

            def t_st(gpsimd, w=w, buf=buf, c0=c0, cw=cw):
                gpsimd.wait_ge(vx.h, w[0])
                gpsimd.wait_ge(ev.h, w[1])
                gpsimd.wait_ge(mm.h, w[2])
                gpsimd.dma_start(out=dbg, in_=buf[:, c0:c0 + cw]).then_inc(st.h, 16)
            st.n += 16
            on("gpsimd", t_st)
            if not keep:
                raise _StopEmission()

        def proj(stat_buf, hi_off, lo_off, w_pair, bias, w_stat, w_strm=None):
            if bias is not None:
                w_mm0 = mm.n

                def pb_ld(sync, bias=bias, w_mm0=w_mm0):
                    sync.wait_ge(mm.h, w_mm0)
                    sync.dma_start(out=B3[0:1, 16384:16384 + H],
                                   in_=bias_sc[bias[0]:bias[0] + 1, :]).then_inc(ld.h, 16)
                    sync.dma_start(out=B3[0:1, 18432:18432 + H],
                                   in_=bias_sc[bias[1]:bias[1] + 1, :]).then_inc(ld.h, 16)
                ld.n += 32
                on("sync", pb_ld)
            bias_ld = ld.n

            for ob in range(H // 512):
                w_mm0 = mm.n

                def pw_ld(sync, w_pair=w_pair, ob=ob, w_mm0=w_mm0, w_strm=w_strm):
                    sync.wait_ge(mm.h, w_mm0)
                    if w_strm is not None:
                        sync.wait_ge(w_strm[0], w_strm[1])
                    for part in range(2):
                        src = w_pair[part]
                        for hc in range(HC):
                            sync.dma_start(
                                out=B1[:, part * 8192 + hc * 512:part * 8192 + (hc + 1) * 512],
                                in_=src[hc * 128:(hc + 1) * 128, ob * 512:(ob + 1) * 512],
                            ).then_inc(ld.h, 16)
                ld.n += 32 * 16
                on("sync", pw_ld)
                w_ld = ld.n

                for tc in range(NC):
                    pi = (ob * NC + tc) % 4
                    w_fr = ps_free[pi]

                    def p_mm(tensor, ob=ob, tc=tc, pi=pi, w_ld=w_ld, w_fr=w_fr,
                             bias=bias, bias_ld=bias_ld, hi_off=hi_off, lo_off=lo_off,
                             stat_buf=stat_buf, w_stat=w_stat):
                        tensor.wait_ge(ld.h, w_ld)
                        tensor.wait_ge(w_fr[0], w_fr[1])
                        if w_stat is not None:
                            tensor.wait_ge(w_stat[0], w_stat[1])
                        last = None
                        for hc in range(HC):
                            so = hc * TQ + tc * 128
                            sh = stat_buf[:, hi_off + so:hi_off + so + 128]
                            sl = stat_buf[:, lo_off + so:lo_off + so + 128]
                            wh = B1[:, hc * 512:(hc + 1) * 512]
                            wl = B1[:, 8192 + hc * 512:8192 + (hc + 1) * 512]
                            tensor.matmul(PS[pi][:], sh, wh, start=(hc == 0), stop=False)
                            tensor.matmul(PS[pi][:], sl, wh, start=False, stop=False)
                            is_last = (hc == HC - 1) and bias is None
                            last = tensor.matmul(PS[pi][:], sh, wl, start=False, stop=is_last)
                        if bias is not None:
                            tensor.wait_ge(ld.h, bias_ld)
                            tensor.matmul(PS[pi][:], ONESB[:],
                                          B3[0:1, 16384 + ob * 512:16384 + (ob + 1) * 512],
                                          start=False, stop=False)
                            last = tensor.matmul(PS[pi][:], ONESB[:],
                                                 B3[0:1, 18432 + ob * 512:18432 + (ob + 1) * 512],
                                                 start=False, stop=True)
                        last.then_inc(mm.h, 1)
                    mm.n += 1
                    on("tensor", p_mm)
                    w_mmn = mm.n

                    def p_ev(scalar, ob=ob, tc=tc, pi=pi, w_mmn=w_mmn):
                        scalar.wait_ge(mm.h, w_mmn)
                        scalar.copy(F2[:, tc * H + ob * 512:tc * H + (ob + 1) * 512],
                                    PS[pi][:]).then_inc(ev.h, 1)
                    ev.n += 1
                    on("scalar", p_ev)
                    ps_free[pi] = (ev.h, ev.n)

        def split_tile(src_buf, src_off, width, dst_buf, hi_off, lo_off, waits):
            def s_sp(vector, waits=waits, src_buf=src_buf, src_off=src_off,
                     width=width, dst_buf=dst_buf, hi_off=hi_off, lo_off=lo_off):
                for sh, n in waits:
                    vector.wait_ge(sh, n)
                vector.tensor_copy(dst_buf[:, hi_off:hi_off + width],
                                   src_buf[:, src_off:src_off + width])
                vector.tensor_tensor(dst_buf[:, lo_off:lo_off + width],
                                     src_buf[:, src_off:src_off + width],
                                     dst_buf[:, hi_off:hi_off + width],
                                     ALU.subtract).then_inc(vx.h, 1)
            vx.n += 1
            on("vector", s_sp)

        def fm_transpose(src_buf, s_hi, s_lo, dst_buf, d_hi, d_lo, w_vx, extra=None):
            def t_tp(scalar, w_vx=w_vx, src_buf=src_buf, s_hi=s_hi, s_lo=s_lo,
                     dst_buf=dst_buf, d_hi=d_hi, d_lo=d_lo, extra=extra):
                scalar.wait_ge(vx.h, w_vx)
                if extra is not None:
                    scalar.wait_ge(extra[0], extra[1])
                for so, do in ((s_hi, d_hi), (s_lo, d_lo)):
                    for oc in range(HC):
                        for tc in range(NC):
                            scalar.dma_start_transpose(
                                dst_buf[:, do + oc * TQ + tc * 128:
                                        do + oc * TQ + tc * 128 + 128],
                                src_buf[:, so + tc * H + oc * 128:
                                        so + tc * H + oc * 128 + 128],
                            ).then_inc(sc.h, 16)
            sc.n += 16 * 2 * HC * NC
            on("scalar", t_tp)

        def layernorm(src_buf, src_off, dst_buf, dst_off, w_src, sq_dump):
            for tc in range(NC):
                def l_red(vector, tc=tc, w_src=w_src, src_buf=src_buf, src_off=src_off):
                    if tc == 0:
                        vector.wait_ge(w_src[0], w_src[1])
                    vector.reduce_sum(STAT[:, 8 + tc:9 + tc],
                                      src_buf[:, src_off + tc * H:src_off + (tc + 1) * H],
                                      AX).then_inc(vx.h, 1)
                vx.n += 1
                on("vector", l_red)

                def l_sq(scalar, tc=tc, w_src=w_src, src_buf=src_buf, src_off=src_off,
                         sq_dump=sq_dump):
                    if tc == 0:
                        scalar.wait_ge(w_src[0], w_src[1])
                    scalar.activation(sq_dump[0][:, sq_dump[1]:sq_dump[1] + H],
                                      src_buf[:, src_off + tc * H:src_off + (tc + 1) * H],
                                      AF.Square,
                                      accum_out=STAT[:, 12 + tc:13 + tc]).then_inc(ev.h, 1)
                ev.n += 1
                on("scalar", l_sq)
            w_red_all, w_sq_all = vx.n, ev.n

            def l_stats(vector, w_red_all=w_red_all, w_sq_all=w_sq_all):
                vector.wait_ge(vx.h, w_red_all)
                vector.wait_ge(ev.h, w_sq_all)
                k = w_red_all
                for tc in range(NC):
                    vector.tensor_scalar(MU[:, tc:tc + 1], STAT[:, 8 + tc:9 + tc],
                                         1.0 / H, None, ALU.mult)
                    vector.tensor_scalar(STAT[:, 16 + tc:17 + tc], STAT[:, 12 + tc:13 + tc],
                                         1.0 / H, None, ALU.mult).then_inc(vx.h, 1)
                    k += 1
                    vector.wait_ge(vx.h, k)
                    vector.tensor_tensor(STAT[:, 20 + tc:21 + tc], MU[:, tc:tc + 1],
                                         MU[:, tc:tc + 1], ALU.mult).then_inc(vx.h, 1)
                    k += 1
                    vector.wait_ge(vx.h, k)
                    vector.tensor_tensor(STAT[:, 24 + tc:25 + tc], STAT[:, 16 + tc:17 + tc],
                                         STAT[:, 20 + tc:21 + tc], ALU.subtract).then_inc(vx.h, 1)
                    k += 1
                    vector.wait_ge(vx.h, k)
                    vector.tensor_scalar(STAT[:, 28 + tc:29 + tc], STAT[:, 24 + tc:25 + tc],
                                         EPS_LN, None, ALU.add).then_inc(vx.h, 1)
                    k += 1
            vx.n += 4 * NC
            on("vector", l_stats)
            w_va = vx.n

            def l_sqrt(scalar, w_va=w_va):
                scalar.wait_ge(vx.h, w_va)
                scalar.activation(STAT[:, 32:32 + NC], STAT[:, 28:28 + NC],
                                  AF.Sqrt).then_inc(ev.h, 1)
            ev.n += 1
            on("scalar", l_sqrt)
            w_ea = ev.n

            def l_rc(vector, w_ea=w_ea):
                vector.wait_ge(ev.h, w_ea)
                vector.reciprocal(RSTD[:, 0:NC], STAT[:, 32:32 + NC]).then_inc(vx.h, 1)
            vx.n += 1
            on("vector", l_rc)
            w_rc = vx.n

            def l_app(vector, w_rc=w_rc, src_buf=src_buf, src_off=src_off,
                      dst_buf=dst_buf, dst_off=dst_off):
                vector.wait_ge(vx.h, w_rc)
                for tc in range(NC):
                    vector.tensor_scalar(dst_buf[:, dst_off + tc * H:dst_off + (tc + 1) * H],
                                         src_buf[:, src_off + tc * H:src_off + (tc + 1) * H],
                                         MU[:, tc:tc + 1], RSTD[:, tc:tc + 1],
                                         ALU.subtract, ALU.mult).then_inc(vx.h, 1)
            vx.n += NC
            on("vector", l_app)
            return (vx.h, vx.n)

        def quant_rounds(vector, y_buf, y_off, r_buf, r_off, width):
            vector.tensor_scalar(r_buf[:, r_off:r_off + width],
                                 y_buf[:, y_off:y_off + width], 128.5, None, ALU.add)
            for k in range(7, -1, -1):
                thr = float(2 ** k)
                for sub in range(0, width, 1024):
                    swd = min(1024, width - sub)
                    vector.tensor_scalar(S8[:, 0:swd], r_buf[:, r_off + sub:r_off + sub + swd],
                                         thr, thr, ALU.is_ge, ALU.mult)
                    vector.tensor_tensor(r_buf[:, r_off + sub:r_off + sub + swd],
                                         r_buf[:, r_off + sub:r_off + sub + swd],
                                         S8[:, 0:swd], ALU.subtract)

        try:
         for ti in range(NTILE):
            rt = ti * TQ

            w_b2 = mm.n

            def x_tp(scalar, rt=rt, w_b2=w_b2):
                scalar.wait_ge(mm.h, w_b2)
                for off, src in ((0, xo_hi), (8192, xo_lo)):
                    for hc in range(HC):
                        scalar.dma_start_transpose(
                            B2[:, off + hc * 512:off + (hc + 1) * 512],
                            src[rt:rt + 512, hc * 128:(hc + 1) * 128],
                        ).then_inc(sc.h, 16)
            sc.n += 16 * 2 * HC
            on("scalar", x_tp)

            proj(B2, 0, 8192, wq_fm, (0, 1), (sc.h, sc.n))
            if tap_hit("q", ti):
                emit_tapbuf(F2, 0, 8192)
            split_tile(F2, 0, 8192, B1, 0, 8192, [(ev.h, ev.n)])
            fm_transpose(B1, 0, 8192, B2, 0, 8192, vx.n, extra=(mm.h, mm.n))

            proj(B2, 0, 8192,
                 (wqkv_tm[0][H:2 * H, :], wqkv_tm[1][H:2 * H, :]),
                 None, (sc.h, sc.n), w_strm=(sc.h, sc.n))
            if tap_hit("qk", ti):
                emit_tapbuf(F2, 0, 8192)
            split_tile(F2, 0, 8192, B2, 0, 8192, [(ev.h, ev.n), (mm.h, mm.n)])
            fm_transpose(B2, 0, 8192, B1, 0, 8192, vx.n, extra=(mm.h, mm.n))
            w_qkfm = sc.n

            w_f1 = (st.h, st.n)
            for jb in range(NJB):
                buf = (jb % 2) * 8192
                w_con = (mm.n - NC) if jb >= 2 else mm.n

                def s_xld(sync, jb=jb, buf=buf, w_con=w_con):
                    sync.wait_ge(mm.h, w_con)
                    for po, src in ((0, xf_hi), (4096, xf_lo)):
                        for hc in range(HC):
                            sync.dma_start_transpose(
                                B3[:, buf + po + hc * JB:buf + po + (hc + 1) * JB],
                                src[jb * JB:(jb + 1) * JB, hc * 128:(hc + 1) * 128],
                            ).then_inc(ld.h, 16)
                ld.n += 16 * 2 * HC
                on("sync", s_xld)
                w_ld = ld.n

                for tc in range(NC):
                    pi = (jb * NC + tc) % 4
                    w_fr = ps_free[pi]

                    def s_mm(tensor, jb=jb, tc=tc, pi=pi, buf=buf, w_ld=w_ld,
                             w_fr=w_fr, w_qkfm=w_qkfm):
                        tensor.wait_ge(ld.h, w_ld)
                        tensor.wait_ge(w_fr[0], w_fr[1])
                        tensor.wait_ge(sc.h, w_qkfm)
                        last = None
                        for hc in range(HC):
                            so = hc * TQ + tc * 128
                            qh = B1[:, so:so + 128]
                            ql = B1[:, 8192 + so:8192 + so + 128]
                            xh = B3[:, buf + hc * JB:buf + (hc + 1) * JB]
                            xl = B3[:, buf + 4096 + hc * JB:buf + 4096 + (hc + 1) * JB]
                            tensor.matmul(PS[pi][:, 0:JB], qh, xh, start=(hc == 0), stop=False)
                            tensor.matmul(PS[pi][:, 0:JB], ql, xh, start=False, stop=False)
                            last = tensor.matmul(PS[pi][:, 0:JB], qh, xl, start=False,
                                                 stop=(hc == HC - 1))
                        last.then_inc(mm.h, 1)
                    mm.n += 1
                    on("tensor", s_mm)
                    w_mmn = mm.n

                    def s_ev(scalar, jb=jb, tc=tc, pi=pi, w_mmn=w_mmn, w_f1=w_f1):
                        scalar.wait_ge(mm.h, w_mmn)
                        scalar.wait_ge(w_f1[0], w_f1[1])
                        scalar.activation(F1[:, tc * J + jb * JB:tc * J + (jb + 1) * JB],
                                          PS[pi][:, 0:JB], AF.Copy,
                                          scale=RSQ_H).then_inc(ev.h, 1)
                    ev.n += 1
                    on("scalar", s_ev)
                    ps_free[pi] = (ev.h, ev.n)

            if tap_hit("s", ti):
                emit_tapbuf(F1, 0, 16384)

            w_sev = ev.n

            def sm_max(vector, w_sev=w_sev):
                vector.wait_ge(ev.h, w_sev)
                for tc in range(NC):
                    vector.reduce_max(NMAX[:, tc:tc + 1], F1[:, tc * J:(tc + 1) * J],
                                      AX, negate=True).then_inc(vx.h, 1)
            vx.n += NC
            on("vector", sm_max)
            w_vmax = vx.n

            def sm_exp(scalar, w_vmax=w_vmax):
                scalar.wait_ge(vx.h, w_vmax)
                for tc in range(NC):
                    scalar.activation(F1[:, tc * J:(tc + 1) * J], F1[:, tc * J:(tc + 1) * J],
                                      AF.Exp, bias=NMAX[:, tc:tc + 1],
                                      accum_out=STAT[:, tc:tc + 1]).then_inc(ev.h, 1)
            ev.n += NC
            on("scalar", sm_exp)
            w_exp = ev.n

            def sm_rs(vector, w_exp=w_exp):
                vector.wait_ge(ev.h, w_exp)
                vector.reciprocal(RSUM[:, 0:NC], STAT[:, 0:NC]).then_inc(vx.h, 1)
            vx.n += 1
            on("vector", sm_rs)
            w_rs = vx.n

            def sm_sp(vector, w_rs=w_rs):
                vector.wait_ge(vx.h, w_rs)
                for tc in range(NC):
                    vector.tensor_scalar(B1[:, tc * J:(tc + 1) * J],
                                         F1[:, tc * J:(tc + 1) * J],
                                         RSUM[:, tc:tc + 1], None, ALU.mult)
                    vector.scalar_tensor_tensor(B2[:, tc * J:(tc + 1) * J],
                                                F1[:, tc * J:(tc + 1) * J],
                                                RSUM[:, tc:tc + 1],
                                                B1[:, tc * J:(tc + 1) * J],
                                                ALU.mult, ALU.subtract).then_inc(vx.h, 1)
            vx.n += NC
            on("vector", sm_sp)
            w_psp = vx.n
            if tap_hit("p", ti):
                emit_tapbuf(F1, 0, 16384)

            for jc in range(JCH):
                pp = jc % 2
                w_wr = gp.n - 32 if jc >= 2 else 0

                def p_tp(scalar, jc=jc, pp=pp, w_psp=w_psp, w_wr=w_wr):
                    scalar.wait_ge(vx.h, w_psp)
                    scalar.wait_ge(gp.h, w_wr)
                    for part, srcb in ((0, B1), (1, B2)):
                        for tc in range(NC):
                            scalar.dma_start_transpose(
                                B3[:, 16384 + pp * 2048 + part * 512 + tc * 128:
                                   16384 + pp * 2048 + part * 512 + (tc + 1) * 128],
                                srcb[:, tc * J + jc * 128:tc * J + (jc + 1) * 128],
                            ).then_inc(sc.h, 16)
                sc.n += 16 * 2 * NC
                on("scalar", p_tp)
                w_tp = sc.n

                def p_st(gpsimd, jc=jc, pp=pp, w_tp=w_tp):
                    gpsimd.wait_ge(sc.h, w_tp)
                    for part in range(2):
                        gpsimd.dma_start(
                            out=pfm_sc[part][jc * 128:(jc + 1) * 128, :],
                            in_=B3[:, 16384 + pp * 2048 + part * 512:
                                   16384 + pp * 2048 + (part + 1) * 512],
                        ).then_inc(gp.h, 16)
                gp.n += 32
                on("gpsimd", p_st)
            w_pfm_wr = gp.n

            def p_rld(sync, w_pfm_wr=w_pfm_wr):
                sync.wait_ge(gp.h, w_pfm_wr)
                for part, dstb in ((0, B1), (1, B2)):
                    for seg in range(4):
                        sync.dma_start(
                            out=dstb[:, seg * 4096:(seg + 1) * 4096],
                            in_=dap(pfm_sc[part], seg * 1024 * TQ,
                                    [[TQ, 128], [128 * TQ, 8], [1, TQ]]),
                        ).then_inc(ld.h, 16)
            ld.n += 16 * 8
            on("sync", p_rld)
            w_pfm = ld.n

            for hc in range(HC):
                buf = (hc % 2) * 8192
                w_con = mm.n - 1 if hc >= 2 else 0

                def px_xld(sync, hc=hc, buf=buf, w_con=w_con):
                    sync.wait_ge(mm.h, w_con)
                    for po, src in ((0, xf_hi), (4096, xf_lo)):
                        for seg in range(4):
                            sync.dma_start(
                                out=B3[:, buf + po + seg * 1024:buf + po + (seg + 1) * 1024],
                                in_=dap(src, seg * 1024 * H + hc * 128,
                                        [[H, 128], [128 * H, 8], [1, 128]]),
                            ).then_inc(ld.h, 16)
                ld.n += 16 * 8
                on("sync", px_xld)
                w_xtm = ld.n

                pi = hc % 4
                w_fr = ps_free[pi]

                def px_mm(tensor, hc=hc, buf=buf, pi=pi, w_xtm=w_xtm, w_fr=w_fr,
                          w_pfm=w_pfm):
                    tensor.wait_ge(ld.h, w_xtm)
                    tensor.wait_ge(ld.h, w_pfm)
                    tensor.wait_ge(w_fr[0], w_fr[1])
                    last = None
                    for jc in range(JCH):
                        xh = B3[:, buf + jc * 128:buf + (jc + 1) * 128]
                        xl = B3[:, buf + 4096 + jc * 128:buf + 4096 + (jc + 1) * 128]
                        ph = B1[:, jc * 512:(jc + 1) * 512]
                        pl = B2[:, jc * 512:(jc + 1) * 512]
                        tensor.matmul(PS[pi][:], xh, ph, start=(jc == 0), stop=False)
                        tensor.matmul(PS[pi][:], xl, ph, start=False, stop=False)
                        last = tensor.matmul(PS[pi][:], xh, pl, start=False,
                                             stop=(jc == JCH - 1))
                    last.then_inc(mm.h, 1)
                mm.n += 1
                on("tensor", px_mm)
                w_mmn = mm.n

                def px_ev(scalar, hc=hc, pi=pi, w_mmn=w_mmn):
                    scalar.wait_ge(mm.h, w_mmn)
                    scalar.copy(F2[:, hc * 512:(hc + 1) * 512], PS[pi][:]).then_inc(ev.h, 1)
                ev.n += 1
                on("scalar", px_ev)
                ps_free[pi] = (ev.h, ev.n)

            if tap_hit("px", ti):
                emit_tapbuf(F2, 0, 8192)

            split_tile(F2, 0, 8192, B3, 0, 8192, [(ev.h, ev.n), (mm.h, mm.n)])

            proj(B3, 0, 8192, wv_fm, (2, 3), (vx.h, vx.n))
            if tap_hit("c", ti):
                emit_tapbuf(F2, 0, 8192)

            split_tile(F2, 0, 8192, B2, 0, 8192, [(ev.h, ev.n), (sc.h, sc.n)])
            fm_transpose(B2, 0, 8192, B3, 0, 8192, vx.n, extra=(mm.h, mm.n))

            proj(B3, 0, 8192, wo_fm, (4, 5), (sc.h, sc.n))

            w_f1lo = vx.n

            def r_xld(sync, rt=rt, w_f1lo=w_f1lo, w_st=(st.h, st.n)):
                sync.wait_ge(vx.h, w_f1lo)
                sync.wait_ge(w_st[0], w_st[1])
                sync.dma_start(
                    out=F1[:, 0:8192],
                    in_=dap(xo, rt * H, [[H, 128], [128 * H, 4], [1, H]]),
                ).then_inc(ld.h, 16)
            ld.n += 16
            on("sync", r_xld)
            w_xot = ld.n
            w_oev = ev.n

            def r_add(vector, w_xot=w_xot, w_oev=w_oev):
                vector.wait_ge(ld.h, w_xot)
                vector.wait_ge(ev.h, w_oev)
                for tc in range(NC):
                    vector.tensor_tensor(F1[:, 8192 + tc * H:8192 + (tc + 1) * H],
                                         F2[:, tc * H:(tc + 1) * H],
                                         F1[:, tc * H:(tc + 1) * H],
                                         ALU.add).then_inc(vx.h, 1)
            vx.n += NC
            on("vector", r_add)
            if tap_hit("y1raw", ti):
                emit_tapbuf(F1, 8192, 8192)

            layernorm(F1, 8192, F1, 0, (vx.h, vx.n), (F2, 0))
            if tap_hit("ln1stat", ti):
                emit_tapbuf(STAT, 0, 40)
            if tap_hit("y1", ti):
                emit_tapbuf(F1, 0, 8192)

            w_b1free = mm.n
            for tc in range(NC):
                w_in = vx.n

                def xq_c(vector, tc=tc, w_in=w_in):
                    vector.wait_ge(vx.h, w_in)
                    vector.tensor_reduce(
                        GMX[:, 0:32],
                        bass.AP(F1, tc * H, [[16384, 128], [64, 32], [1, 64]]),
                        AX, ALU.max, apply_absolute_value=True).then_inc(vx.h, 1)
                vx.n += 1
                on("vector", xq_c)
                w1_ = vx.n

                def xq_s(vector, w1_=w1_):
                    vector.wait_ge(vx.h, w1_)
                    vector.tensor_scalar(USC[:, 0:32], GMX[:, 0:32], EPS_Q, None,
                                         ALU.max).then_inc(vx.h, 1)
                    vector.wait_ge(vx.h, w1_ + 1)
                    vector.reciprocal(VSC[:, 0:32], USC[:, 0:32]).then_inc(vx.h, 1)
                    vector.wait_ge(vx.h, w1_ + 2)
                    vector.tensor_scalar(GMX[:, 0:32], VSC[:, 0:32], 127.0, None,
                                         ALU.mult)
                    vector.tensor_scalar(VSC[:, 0:32], USC[:, 0:32], 1.0 / 127.0, None,
                                         ALU.mult).then_inc(vx.h, 1)
                vx.n += 3
                on("vector", xq_s)
                w2_ = vx.n

                def xq_y(vector, tc=tc, w2_=w2_):
                    vector.wait_ge(vx.h, w2_)
                    vector.tensor_tensor(
                        bass.AP(F2, 0, [[8192, 128], [64, 32], [1, 64]]),
                        bass.AP(F1, tc * H, [[16384, 128], [64, 32], [1, 64]]),
                        bass.AP(GMX, 0, [[64, 128], [1, 32], [0, 64]]),
                        ALU.mult)
                    quant_rounds(vector, F2, 0, F2, 2048, H)
                    vector.scalar_tensor_tensor(F2[:, 4096:4096 + H], F2[:, 0:H], 0.5,
                                                F2[:, 2048:2048 + H], ALU.add, ALU.subtract)
                    vector.tensor_tensor(
                        bass.AP(F2, 6144, [[8192, 128], [64, 32], [1, 64]]),
                        bass.AP(F2, 4096, [[8192, 128], [64, 32], [1, 64]]),
                        bass.AP(VSC, 0, [[64, 128], [1, 32], [0, 64]]),
                        ALU.mult)
                    vector.tensor_copy(B2[:, tc * H:tc * H + H], F2[:, 6144:6144 + H])
                    vector.tensor_tensor(B2[:, 8192 + tc * H:8192 + tc * H + H],
                                         F2[:, 6144:6144 + H],
                                         B2[:, tc * H:tc * H + H],
                                         ALU.subtract).then_inc(vx.h, 1)
                vx.n += 1
                on("vector", xq_y)

            w_xq = vx.n

            def xq_tp(scalar, w_xq=w_xq, w_b1free=w_b1free):
                scalar.wait_ge(vx.h, w_xq)
                scalar.wait_ge(mm.h, w_b1free)
                for so, do in ((0, 0), (8192, 8192)):
                    for hc in range(HC):
                        for tc in range(NC):
                            scalar.dma_start_transpose(
                                B1[:, do + hc * 512 + tc * 128:do + hc * 512 + (tc + 1) * 128],
                                B2[:, so + tc * H + hc * 128:so + tc * H + (hc + 1) * 128],
                            ).then_inc(sc.h, 16)
            sc.n += 16 * 2 * HC * NC
            on("scalar", xq_tp)
            w_xqfm = sc.n
            if tap_hit("xq", ti):
                emit_tapbuf(F2, 6144, 2048)

            silu_free = [0, 0]
            hq_prev = w_xq
            hq_tp_prev = 0
            for sl in range(NSL):
                for fb in range(2):
                    fbg = sl * 2 + fb
                    w_con = mm.n

                    def f_sgld(sync, fbg=fbg, w_con=w_con):
                        sync.wait_ge(mm.h, w_con)
                        f0 = fbg * 512
                        for base, src in ((0, sg1_fm), (8192, sg2_fm)):
                            for hc in range(HC):
                                sync.dma_start(
                                    out=B3[:, base + hc * 512:base + (hc + 1) * 512],
                                    in_=src[hc * 128:(hc + 1) * 128, f0:f0 + 512],
                                ).then_inc(ld.h, 16)
                    ld.n += 16 * 2 * HC
                    on("sync", f_sgld)
                    w_sg = ld.n

                    for tc in range(NC):
                        pa = (2 * (fb * NC + tc)) % 4
                        pb = pa + 1 if pa % 2 == 0 else pa - 1
                        pb = (pa + 1) % 4
                        w_fra = ps_free[pa]
                        w_frb = ps_free[pb]

                        def f_mm(tensor, tc=tc, pa=pa, pb=pb, w_fra=w_fra,
                                 w_frb=w_frb, w_sg=w_sg, w_xqfm=w_xqfm):
                            tensor.wait_ge(ld.h, w_sg)
                            tensor.wait_ge(sc.h, w_xqfm)
                            tensor.wait_ge(w_fra[0], w_fra[1])
                            tensor.wait_ge(w_frb[0], w_frb[1])
                            for base, pi in ((0, pa), (8192, pb)):
                                last = None
                                for hc in range(HC):
                                    so = hc * 512 + tc * 128
                                    sh = B1[:, so:so + 128]
                                    slo = B1[:, 8192 + so:8192 + so + 128]
                                    wt = B3[:, base + hc * 512:base + (hc + 1) * 512]
                                    tensor.matmul(PS[pi][:], sh, wt, start=(hc == 0), stop=False)
                                    last = tensor.matmul(PS[pi][:], slo, wt, start=False,
                                                         stop=(hc == HC - 1))
                                last.then_inc(mm.h, 1)
                        mm.n += 2
                        on("tensor", f_mm)
                        w_ab = mm.n

                        par = 0
                        w_sif = silu_free[par]

                        def f_silu(scalar, pa=pa, par=par, w_ab=w_ab, w_sif=w_sif):
                            scalar.wait_ge(mm.h, w_ab - 1)
                            scalar.wait_ge(vx.h, w_sif)
                            scalar.activation(S8[:, 1024:1536],
                                              PS[pa][:], AF.Silu,
                                              scale=SW[:, 0:1]).then_inc(ev.h, 1)
                        ev.n += 1
                        on("scalar", f_silu)
                        w_si = ev.n
                        ps_free[pa] = (ev.h, ev.n)

                        def f_h(vector, tc=tc, fb=fb, pb=pb, par=par, w_si=w_si,
                                w_ab=w_ab, hq_prev=hq_prev):
                            vector.wait_ge(ev.h, w_si)
                            vector.wait_ge(mm.h, w_ab)
                            vector.wait_ge(vx.h, hq_prev)
                            vector.scalar_tensor_tensor(
                                F2[:, tc * 1024 + fb * 512:tc * 1024 + (fb + 1) * 512],
                                PS[pb][:], SW[:, 1:2],
                                S8[:, 1024:1536],
                                ALU.mult, ALU.mult).then_inc(vx.h, 1)
                        vx.n += 1
                        on("vector", f_h)
                        ps_free[pb] = (vx.h, vx.n)
                        silu_free[par] = vx.n

                w_h = vx.n

                def hq_c(vector, w_h=w_h):
                    vector.wait_ge(vx.h, w_h)
                    vector.tensor_reduce(
                        GMX[:, 0:64],
                        bass.AP(F2, 0, [[8192, 128], [64, 64], [1, 64]]),
                        AX, ALU.max, apply_absolute_value=True).then_inc(vx.h, 1)
                vx.n += 1
                on("vector", hq_c)
                w1_ = vx.n

                def hq_s(vector, w1_=w1_):
                    vector.wait_ge(vx.h, w1_)
                    vector.tensor_scalar(USC[:, 0:64], GMX[:, 0:64], EPS_Q, None,
                                         ALU.max).then_inc(vx.h, 1)
                    vector.wait_ge(vx.h, w1_ + 1)
                    vector.reciprocal(VSC[:, 0:64], USC[:, 0:64]).then_inc(vx.h, 1)
                    vector.wait_ge(vx.h, w1_ + 2)
                    vector.tensor_scalar(GMX[:, 0:64], VSC[:, 0:64], 127.0, None,
                                         ALU.mult)
                    vector.tensor_scalar(VSC[:, 0:64], USC[:, 0:64], 1.0 / 127.0, None,
                                         ALU.mult).then_inc(vx.h, 1)
                vx.n += 3
                on("vector", hq_s)
                w2_ = vx.n

                def hq_y(vector, w2_=w2_, hq_tp_prev=hq_tp_prev):
                    vector.wait_ge(vx.h, w2_)
                    vector.wait_ge(sc.h, hq_tp_prev)
                    vector.tensor_tensor(
                        bass.AP(F2, 4096, [[8192, 128], [64, 64], [1, 64]]),
                        bass.AP(F2, 0, [[8192, 128], [64, 64], [1, 64]]),
                        bass.AP(GMX, 0, [[64, 128], [1, 64], [0, 64]]),
                        ALU.mult)
                    quant_rounds(vector, F2, 4096, F2, 0, 4096)
                    vector.scalar_tensor_tensor(F2[:, 0:4096], F2[:, 4096:8192], 0.5,
                                                F2[:, 0:4096], ALU.add, ALU.subtract)
                    vector.tensor_tensor(
                        bass.AP(F2, 4096, [[8192, 128], [64, 64], [1, 64]]),
                        bass.AP(F2, 0, [[8192, 128], [64, 64], [1, 64]]),
                        bass.AP(VSC, 0, [[64, 128], [1, 64], [0, 64]]),
                        ALU.mult)
                    vector.tensor_copy(B2[:, 0:4096], F2[:, 4096:8192])
                    vector.tensor_tensor(B2[:, 4096:8192], F2[:, 4096:8192],
                                         B2[:, 0:4096], ALU.subtract).then_inc(vx.h, 1)
                vx.n += 1
                on("vector", hq_y)
                w_hq = vx.n

                w_b2fm = mm.n

                def hq_tp(scalar, w_hq=w_hq, w_b2fm=w_b2fm):
                    scalar.wait_ge(vx.h, w_hq)
                    scalar.wait_ge(mm.h, w_b2fm)
                    for so, do in ((0, 8192), (4096, 12288)):
                        for fc in range(FCS):
                            for tc in range(NC):
                                scalar.dma_start_transpose(
                                    B2[:, do + fc * 512 + tc * 128:
                                       do + fc * 512 + (tc + 1) * 128],
                                    B2[:, so + tc * 1024 + fc * 128:
                                       so + tc * 1024 + (fc + 1) * 128],
                                ).then_inc(sc.h, 16)
                sc.n += 16 * 2 * FCS * NC
                on("scalar", hq_tp)
                w_hqfm = sc.n
                hq_tp_prev = sc.n
                hq_prev = w_hq

                for mb in range(4):
                    w_con = mm.n

                    def f3_ld(sync, sl=sl, mb=mb, w_con=w_con):
                        sync.wait_ge(mm.h, w_con)
                        for fc in range(FCS):
                            sync.dma_start(
                                out=B3[:, 16384 + fc * 512:16384 + (fc + 1) * 512],
                                in_=sg3_fm[sl * FSL + fc * 128:sl * FSL + (fc + 1) * 128,
                                           mb * 512:(mb + 1) * 512],
                            ).then_inc(ld.h, 16)
                    ld.n += 16 * FCS
                    on("sync", f3_ld)
                    w_s3 = ld.n
                    w_fr = pbig_free[0]

                    def f3_mm(tensor, mb=mb, w_s3=w_s3, w_fr=w_fr, w_hqfm=w_hqfm):
                        tensor.wait_ge(ld.h, w_s3)
                        tensor.wait_ge(sc.h, w_hqfm)
                        tensor.wait_ge(w_fr[0], w_fr[1])
                        for tc in range(NC):
                            last = None
                            for fc in range(FCS):
                                so = fc * 512 + tc * 128
                                sh = B2[:, 8192 + so:8192 + so + 128]
                                slo = B2[:, 12288 + so:12288 + so + 128]
                                wt = B3[:, 16384 + fc * 512:16384 + (fc + 1) * 512]
                                tensor.matmul(PBIG[:, tc * 512:(tc + 1) * 512], sh, wt,
                                              start=(fc == 0), stop=False)
                                last = tensor.matmul(PBIG[:, tc * 512:(tc + 1) * 512],
                                                     slo, wt, start=False,
                                                     stop=(fc == FCS - 1))
                            last.then_inc(mm.h, 1)
                    mm.n += 4
                    on("tensor", f3_mm)
                    w_f3 = mm.n

                    def f3_acc(vector, sl=sl, mb=mb, w_f3=w_f3):
                        vector.wait_ge(mm.h, w_f3)
                        o_ap = bass.AP(F1, 8192 + mb * 512, [[16384, 128], [2048, 4], [1, 512]])
                        i_ap = bass.AP(PBIG, 0, [[2048, 128], [512, 4], [1, 512]])
                        if sl == 0:
                            vector.tensor_copy(o_ap, i_ap).then_inc(vx.h, 1)
                        else:
                            vector.tensor_tensor(o_ap, o_ap, i_ap, ALU.add).then_inc(vx.h, 1)
                    vx.n += 1
                    on("vector", f3_acc)
                    pbig_free[0] = (vx.h, vx.n)

            w_cac = vx.n

            def y2_raw(vector, w_cac=w_cac):
                vector.wait_ge(vx.h, w_cac)
                vector.scalar_tensor_tensor(F2[:, 0:8192], F1[:, 8192:16384],
                                            SW[:, 2:3], F1[:, 0:8192],
                                            ALU.mult, ALU.add).then_inc(vx.h, 1)
            vx.n += 1
            on("vector", y2_raw)
            if tap_hit("y2raw", ti):
                emit_tapbuf(F2, 0, 8192)

            layernorm(F2, 0, F1, 0, (vx.h, vx.n), (F1, 8192))
            if tap_hit("y2", ti):
                emit_tapbuf(F1, 0, 8192)

            w_y2 = vx.n

            def o_st(gpsimd, rt=rt, w_y2=w_y2):
                gpsimd.wait_ge(vx.h, w_y2)
                gpsimd.dma_start(
                    out=dap(out, rt * H, [[H, 128], [128 * H, 4], [1, H]]),
                    in_=F1[:, 0:8192]).then_inc(st.h, 16)
            st.n += 16
            on("gpsimd", o_st)

        except _StopEmission:
            pass

        w_all = st.n

        def fin(gpsimd, w_all=w_all):
            gpsimd.wait_ge(st.h, w_all)
        on("gpsimd", fin)

        @block.sync
        def _(sync):
            for fn in steps["sync"]:
                fn(sync)

        @block.gpsimd
        def _(gpsimd):
            for fn in steps["gpsimd"]:
                fn(gpsimd)

        @block.vector
        def _(vector):
            for fn in steps["vector"]:
                fn(vector)

        @block.scalar
        def _(scalar):
            for fn in steps["scalar"]:
                fn(scalar)

        @block.tensor
        def _(tensor):
            for fn in steps["tensor"]:
                fn(tensor)

    es.close()
    return nc


def make_in_maps(hs, wqkv, bqkv, wo, bo, w1, w2, w3):
    in_maps = []
    for c in range(N_CORES):
        b, half = c // 2, c % 2
        in_maps.append({
            "xo": np.ascontiguousarray(hs[b, half * T:(half + 1) * T, :]),
            "xf": np.ascontiguousarray(hs[b]),
            "wqkv": wqkv, "bqkv": bqkv, "wout": wo, "bout": bo,
            "w1": w1, "w2": w2, "w3": w3,
        })
    return in_maps


def kernel(L0_attn_hidden_states, L0_attn_w_qkv, L0_attn_b_qkv, L0_attn_w_out,
           L0_attn_b_out, L0_ffn_w1, L0_ffn_w2, L0_ffn_w3):
    hs = np.ascontiguousarray(np.asarray(L0_attn_hidden_states, dtype=np.float32))
    wqkv = np.ascontiguousarray(np.asarray(L0_attn_w_qkv, dtype=np.float32))
    bqkv = np.ascontiguousarray(np.asarray(L0_attn_b_qkv, dtype=np.float32))
    wo = np.ascontiguousarray(np.asarray(L0_attn_w_out, dtype=np.float32))
    bo = np.ascontiguousarray(np.asarray(L0_attn_b_out, dtype=np.float32))
    w1 = np.ascontiguousarray(np.asarray(L0_ffn_w1, dtype=np.float32))
    w2 = np.ascontiguousarray(np.asarray(L0_ffn_w2, dtype=np.float32))
    w3 = np.ascontiguousarray(np.asarray(L0_ffn_w3, dtype=np.float32))

    nc = build()
    res = run_bass_kernel_spmd(nc, make_in_maps(hs, wqkv, bqkv, wo, bo, w1, w2, w3),
                               core_ids=list(range(N_CORES)))
    full = np.empty((B, S, H), dtype=np.float32)
    for c in range(N_CORES):
        b, half = c // 2, c % 2
        full[b, half * T:(half + 1) * T, :] = res.results[c]["out"]
    return full
